# version 17
# speedup vs baseline: 1.1798x; 1.0089x over previous
import numpy as np
import ml_dtypes
import concourse.bass as bass
import concourse.mybir as mybir
from concourse.bass_utils import run_bass_kernel_spmd

F32 = mybir.dt.float32
BF16 = mybir.dt.bfloat16
F16 = mybir.dt.float16
I32 = mybir.dt.int32
U8 = mybir.dt.uint8
I16 = mybir.dt.int16
AF = mybir.ActivationFunctionType
OP = mybir.AluOpType
AX = mybir.AxisListType.X

f32 = np.float32

B, T, M, D, G, H = 4, 2048, 2048, 128, 4, 4
NHEAD = G * H
I = 5632
GU_ROWS = 2 * I
GU_W = M // 8
DW_W = I // 8
NCH = M // 128
TOWN = T // 2
EPS_LN = f32(1e-5)
EPS_RMS = f32(1e-6)

ENGS = ("tensor", "vector", "scalar", "gpsimd", "sync")

SEMS = [
    "sCst", "sRope", "sXin0", "sXin1", "sMod0", "sMod1", "sWkv0", "sWkv1",
    "sWo", "sPk", "sPk1", "sMsk", "sGuSt0", "sGuSt1", "sGuSc0", "sGuSc1", "sDwSt0", "sDwSt1", "sDwSt2", "sDwSt3",
    "sX2w0", "sX2w1", "sX2r0", "sX2r1", "sXr0", "sXr1", "sOut", "sDwr0", "sDwr1",
    "sDbg0", "sDbg1", "sOutF0", "sOutF1",
    "cV", "cS", "cP", "cT",
]

ESEM = {"vector": "cV", "scalar": "cS", "gpsimd": "cP", "tensor": "cT"}


class Prog:
    def __init__(self, nc):
        self.nc = nc
        self.q = {e: [] for e in ENGS}
        self.sems = {}
        self.cnt = {}

    def add_sem(self, name, handle):
        self.sems[name] = handle
        self.cnt[name] = 0

    def on(self, eng, fn):
        self.q[eng].append(fn)

    def wait(self, eng, name, val):
        if val is None or val <= 0:
            return
        h = self.sems[name]
        self.on(eng, lambda E, h=h, v=val: E.wait_ge(h, v))

    def op(self, eng, fn, sem=None):
        name = ESEM[eng]
        if eng != "tensor":
            self.wait(eng, name, self.cnt[name])
        v = self.cnt[name] = self.cnt[name] + 1
        h = self.sems[name]
        self.on(eng, lambda E, fn=fn, h=h: fn(E).then_inc(h, 1))
        return v

    def dma(self, eng, out, in_, sem, **kw):
        v = self.cnt[sem] = self.cnt[sem] + 16
        h = self.sems[sem]
        self.on(eng, lambda E, o=out, i=in_, h=h, kw=kw: E.dma_start(out=o, in_=i, **kw).then_inc(h, 16))
        return v

    def flush(self, block):
        for eng in ENGS:
            ops = self.q[eng]
            if not ops:
                continue
            getattr(block, eng)(lambda E, ops=ops: [f(E) for f in ops] and None)
        self.q = {e: [] for e in ENGS}


def perm_kmajor(n):
    c = np.arange(n)
    pw = n // 4
    kk = c // pw
    g = (c % pw) // 32
    rl = c % 32
    return (128 * g + 4 * rl + kk).astype(np.int64)


def perm_dwq(n):
    c = np.arange(n)
    q = c // 1408
    cl = c % 1408
    kk = cl // 352
    g = 11 * q + (cl % 352) // 32
    rl = cl % 32
    return (128 * g + 4 * rl + kk).astype(np.int64)


def rope_tables(positions):
    qk = (D ** -0.25)
    half = D // 2
    freqs = 10000.0 ** (-np.arange(half, dtype=np.float64) / half)
    ang = positions[None, :].astype(np.float64) * freqs[:, None]
    c, s = np.cos(ang), np.sin(ang)
    Ct = np.concatenate([c, c], axis=0) * qk
    St = np.concatenate([-s, s], axis=0) * qk
    return Ct.astype(np.float32), St.astype(np.float32)


def causal_masks(perm):
    out = np.zeros((2, 16, 128, 512), dtype=np.uint8)
    for qb in range(2):
        gq = perm[qb * 512:(qb + 1) * 512]
        for cc in range(16):
            gk = perm[cc * 128:(cc + 1) * 128]
            out[qb, cc] = (gk[:, None] <= gq[None, :]).astype(np.uint8)
    return out


def build_program(upto=3, dbg=()):
    nc = bass.Bass("TRN2", target_bir_lowering=False)
    dbg = set(dbg)

    xb = nc.dram_tensor("xb", [T, M], F32, kind="ExternalInput")
    wq = nc.dram_tensor("wq", [NHEAD, M, D], F32, kind="ExternalInput")
    wk = nc.dram_tensor("wk", [G, M, D], F32, kind="ExternalInput")
    wv = nc.dram_tensor("wv", [G, M, D], F32, kind="ExternalInput")
    wo = nc.dram_tensor("wo", [NHEAD, D, M], F32, kind="ExternalInput")
    gup = nc.dram_tensor("gup", [GU_ROWS, GU_W], I32, kind="ExternalInput")
    gusT = nc.dram_tensor("gusT", [GU_ROWS, 16], F16, kind="ExternalInput")
    dwp = nc.dram_tensor("dwp", [M, DW_W], I32, kind="ExternalInput")
    dwsT = nc.dram_tensor("dwsT", [M, 44], F16, kind="ExternalInput")
    cT = nc.dram_tensor("cT", [128, 16], F32, kind="ExternalInput")
    wmod = nc.dram_tensor("wmod", [M, 6 * M], F32, kind="ExternalInput")
    rkC = nc.dram_tensor("rkC", [128, T], F32, kind="ExternalInput")
    rkS = nc.dram_tensor("rkS", [128, T], F32, kind="ExternalInput")
    masks = nc.dram_tensor("masks", [2 * 16 * 128, 512], U8, kind="ExternalInput")
    identb = nc.dram_tensor("identb", [128, 128], BF16, kind="ExternalInput")
    onesb = nc.dram_tensor("onesb", [128, 128], BF16, kind="ExternalInput")
    pswap = nc.dram_tensor("pswap", [128, 128], BF16, kind="ExternalInput")

    xout = nc.dram_tensor("xout", [TOWN, M], F32, kind="ExternalOutput")
    DBG = {}
    for name, shape, dt in [
        ("d_h1", [T, M], BF16), ("d_kall", [128, G * T], BF16),
        ("d_vall", [128, G * T], BF16), ("d_qall", [128, NHEAD * TOWN], BF16),
        ("d_x2", [TOWN, M], F32), ("d_oall", [128, NHEAD * TOWN], BF16),
        ("d_bc", [128, 6 * M], BF16), ("d_dgu", [GU_ROWS, M], BF16),
        ("d_ddw", [M, I], BF16), ("d_act", [128, 2 * 44 * 512], BF16),
        ("d_nrm", [1, 32], F32),
    ]:
        if name in dbg:
            DBG[name] = nc.dram_tensor(name, shape, dt, kind="ExternalOutput")

    x2d = nc.dram_tensor("x2d", [TOWN, M], F32)
    bcd = nc.dram_tensor("bcd", [3, 128, M], BF16)
    dgu = nc.dram_tensor("dgu", [GU_ROWS, M], BF16)
    ddw = nc.dram_tensor("ddw", [M, I], BF16)

    P = Prog(nc)
    import contextlib
    ctx = contextlib.ExitStack()
    ctx_dq = contextlib.ExitStack()
    ctx_attn = contextlib.ExitStack()
    ctx1 = contextlib.ExitStack()
    ctx2 = contextlib.ExitStack()
    ctx3 = contextlib.ExitStack()
    with ctx:
        for s in SEMS:
            P.add_sem(s, ctx.enter_context(nc.semaphore(s)))

        sb = lambda name, shape, dt: ctx.enter_context(nc.sbuf_tensor(name, shape, dt))
        cst_id = sb("cst_id", [128, 128], BF16)
        cst_on = sb("cst_on", [128, 128], BF16)
        cst_pw = sb("cst_pw", [128, 128], BF16)
        bc = {}
        bcst = sb("bcst", [128, 256], BF16)
        srow = sb("srow", [128, 512], F32)
        sbrow = sb("sbrow", [128, 256], BF16)
        stat = sb("stat", [128, 24], F32)
        nrm = sb("nrm", [1, 32], F32)
        ntmp = sb("ntmp", [1, 2], F32)
        nBcol = sb("nBcol", [128, 1], F32)
        ctb = sb("ctb", [128, 16], F32)
        scT = sb("scT", [128, 16], BF16)
        xin = [sb("xin%d" % i, [128, M], F32) for i in range(2)]
        sqs = sb("sqs", [128, M], F32)
        ht = {"b": None}
        sbd = lambda name, shape, dt: ctx_dq.enter_context(nc.sbuf_tensor(name, shape, dt, side="right"))
        dq_pk0 = sbd("dq_pk", [128, GU_W], I32)
        dq_pk = [dq_pk0, dq_pk0]
        dq_pk20 = sbd("dq_pk2", [128, DW_W], I32)
        dq_pk2 = [dq_pk20, dq_pk20]
        dq_m = [sbd("dq_m%d" % i, [128, 2048], I16) for i in range(2)]
        dq_s0 = sbd("dq_s", [128, 2048], I16)
        dq_s = [dq_s0, dq_s0]
        dq_sg = [sbd("dq_sg%d" % i, [128, 2048], BF16) for i in range(2)]
        dq_r4 = sbd("dq_r4", [128, 2048], BF16)
        dq_r6 = sbd("dq_r6", [128, 2048], BF16)
        dq_r46 = [sbd("dq_r46%d" % i, [128, 2048], BF16) for i in range(2)]
        dq_t1 = dq_r6
        dq_w = [sbd("dq_w%d" % i, [128, 2048], BF16) for i in range(2)]
        dq_scb = [sbd("dq_scb%d" % i, [128, 44], BF16) for i in range(2)]
        dq_scw = [sbd("dq_scw%d" % i, [128, 44], F16) for i in range(2)]
        sba = lambda name, shape, dt: ctx_attn.enter_context(nc.sbuf_tensor(name, shape, dt))
        sb1 = lambda name, shape, dt: ctx1.enter_context(nc.sbuf_tensor(name, shape, dt))
        bc["ga"] = sba("bc_ga", [128, M], BF16)
        q_all = sba("q_all", [128, NHEAD * TOWN], BF16)
        k_all = sba("k_all", [128, G * T], BF16)
        v_all = sba("v_all", [128, G * T], BF16)
        bc["ca1"] = sb1("bc_ca1", [128, M], BF16)
        bc["sa"] = sb1("bc_sa", [128, M], BF16)
        ht["b"] = [sb1("h1t%d" % i, [128, M], BF16) for i in range(2)]
        rbC = sb1("rbC", [128, 512], F32)
        rbS = sb1("rbS", [128, 512], F32)
        h1fm = sb1("h1fm", [128, NCH * 512], BF16)
        wbuf = [sb1("wbuf%d" % i, [128, NCH * 128], BF16) for i in range(2)]
        wmodb = [sb1("wmodb%d" % i, [128, NCH * 128], BF16) for i in range(2)]
        rope_bf = sb1("rope_bf", [128, 512], BF16)
        rb2 = sb1("rb2", [128, 512], BF16)

        pbk = [ctx.enter_context(nc.psum_tensor("pb%d" % i, [128, 512], F32)) for i in range(8)]
        pT = [pbk[4][:, 0:64].bitcast(BF16), pbk[5][:, 0:64].bitcast(BF16)]

        P.dma("sync", cst_id[:, :], identb[:, :], "sCst")
        P.dma("sync", cst_on[:, :], onesb[:, :], "sCst")
        P.dma("sync", cst_pw[:, :], pswap[:, :], "sCst")
        P.dma("sync", ctb[:, :], cT[:, :], "sCst")
        vCst = P.cnt["sCst"]
        P.wait("scalar", "sCst", vCst)
        vSiluC = P.op("scalar", lambda E: E.activation(scT[:, :], ctb[:, :], AF.Silu), sem="cS")
        P.op("vector", lambda E: E.memset(nrm[:, :], 0.0), sem="cV")

        VEC_ORDER = ["sa", "ca1", "ga", "sbv", "cb1", "gb"]
        mst = {"vP": [0, 0], "vEvac": [0, 0], "vBcE": [0, 0], "vSp": [0, 0], "pend": None}

        def load_mod(jt):
            half = jt % 2
            j0 = jt * 128
            srcw = wmod[:, j0:j0 + 128].rearrange("(c p) j -> p c j", p=128)
            dst = wmodb[half][:, :].rearrange("p (c j) -> p c j", c=NCH)
            P.wait("gpsimd", "cT", mst["vP"][half])
            return P.dma("gpsimd", dst, srcw, "sMod%d" % half)

        def emit_mod_tile(jt, vL):
            half = jt % 2
            buf = wmodb[half]
            P.wait("tensor", "sMod%d" % half, vL)
            P.wait("tensor", "cS", vSiluC)
            P.wait("tensor", "cV", mst["vEvac"][half])

            def mm(E, buf=buf, half=half):
                r = None
                for c in range(NCH):
                    r = E.matmul(pbk[6][0:1, half * 128:half * 128 + 128], scT[:, c:c + 1],
                                 buf[:, c * 128:(c + 1) * 128], start=(c == 0), stop=(c == NCH - 1))
                return r
            vP = P.op("tensor", mm, sem="cT")
            mst["vP"][half] = vP
            vec = VEC_ORDER[jt // 16]
            P.wait("vector", "cT", vP)
            rsl = srow[0:1, half * 128:half * 128 + 128]
            P.op("vector", lambda E, rsl=rsl, half=half: E.tensor_copy(
                rsl, pbk[6][0:1, half * 128:half * 128 + 128]))
            if vec in ("ca1", "cb1"):
                P.op("vector", lambda E, rsl=rsl: E.tensor_scalar(rsl, rsl, 1.0, None, OP.add))
            elif vec in ("ga", "gb"):
                asl = srow[0:1, 256 + half * 128:256 + half * 128 + 128]
                vPre = P.op("vector", lambda E, rsl=rsl: E.tensor_copy(rsl, rsl), sem="cV")
                P.wait("scalar", "cV", vPre)
                vSg = P.op("scalar", lambda E, rsl=rsl, asl=asl: E.activation(asl, rsl, AF.Sign),
                           sem="cS")
                P.wait("vector", "cS", vSg)
                P.op("vector", lambda E, rsl=rsl, asl=asl: E.tensor_tensor(rsl, rsl, asl, OP.mult))
                P.op("vector", lambda E, rsl=rsl: E.tensor_scalar(rsl, rsl, 0.25, None, OP.max))
                P.op("vector", lambda E, rsl=rsl, asl=asl: E.tensor_tensor(rsl, rsl, asl, OP.mult))
            brow = sbrow[0:1, half * 128:half * 128 + 128]
            vB = P.op("vector", lambda E, rsl=rsl, brow=brow: E.tensor_copy(brow, rsl), sem="cV")
            mst["vEvac"][half] = vB
            j0r = (jt % 16) * 128
            P.wait("tensor", "cV", vB)
            P.wait("tensor", "cV", mst["vBcE"][half])
            psl = pbk[7][:, half * 128:half * 128 + 128]

            def bcast(E, brow=brow, psl=psl):
                return E.matmul(psl, cst_on[0:1, :], brow, start=True, stop=True)
            vBc = P.op("tensor", bcast, sem="cT")
            P.wait("vector", "cT", vBc)
            if vec in ("sbv", "cb1", "gb"):
                vi = ("sbv", "cb1", "gb").index(vec)
                ssl = bcst[:, half * 128:half * 128 + 128]
                P.wait("vector", "sDbg%d" % half, mst["vSp"][half])
                vBcE = P.op("vector", lambda E, ssl=ssl, psl=psl: E.tensor_copy(ssl, psl),
                            sem="cV")
                P.wait("gpsimd", "cV", vBcE)
                mst["vSp"][half] = P.dma("gpsimd", bcd[vi, :, j0r:j0r + 128], ssl,
                                         "sDbg%d" % half)
            else:
                tgt = bc[vec]
                vBcE = P.op("vector", lambda E, tgt=tgt, j0r=j0r, psl=psl: E.tensor_copy(
                    tgt[:, j0r:j0r + 128], psl), sem="cV")
            mst["vBcE"][half] = vBcE
            return vBcE

        mst["pend"] = load_mod(0)
        for jt in range(32):
            vLm, mst["pend"] = mst["pend"], None
            if jt + 1 < 32:
                mst["pend"] = load_mod(jt + 1)
            vBcSaCa = emit_mod_tile(jt, vLm)

        dqst = {"i": 0, "vW_dma": [0, 0], "vLoad": [0, 0], "vWop": [0, 0],
                "vAct": [0, 0], "vP46": [0, 0], "vT1": [0, 0], "vX": [0, 0],
                "vScCp": [0, 0], "vSgnP": [0, 0], "wmark": [], "cur": None, "lagB": []}

        def dq_desc(idx):
            db = idx % 2
            if idx < 88:
                kind, tile, quarter = "gu", idx, 0
            else:
                kind, tile, quarter = "dw", (idx - 88) // 4, (idx - 88) % 4
            return dict(
                idx=idx, db=db, kind=kind, tile=tile, q=quarter,
                sPkN="sPk" if db == 0 else "sPk1", sDwrN="sDwr%d" % db,
                sb_i=db if kind == "gu" else (tile % 2),
                r0=tile * 128, ncols=2048 if kind == "gu" else 1408,
                ngl=16 if kind == "gu" else 11)

        def emit_dqA(d):
            db, kind, sb_i, tile = d["db"], d["kind"], d["sb_i"], d["tile"]
            if kind == "gu" or d["q"] == 0:
                P.wait("sync", "cV", max(dqst["vX"]))
                P.wait("sync", "cV", dqst["vScCp"][sb_i])
                if kind == "gu":
                    P.dma("sync", dq_pk[db][:, :], gup[tile * 128:tile * 128 + 128, :],
                          d["sPkN"])
                    vBoth = P.dma("sync", dq_scw[sb_i][:, 0:16],
                                  gusT[tile * 128:tile * 128 + 128, :], d["sPkN"])
                else:
                    P.dma("sync", dq_pk2[db][:, :], dwp[tile * 128:tile * 128 + 128, :],
                          d["sPkN"])
                    vBoth = P.dma("sync", dq_scw[sb_i][:, 0:44],
                                  dwsT[tile * 128:tile * 128 + 128, :], d["sPkN"])
                dqst["vLoad"][db] = vBoth
            P.wait("vector", d["sPkN"], dqst["vLoad"][db])
            if kind == "gu" or d["q"] == 0:
                sc_n = 16 if kind == "gu" else 44
                dqst["vScCp"][sb_i] = P.op("vector", lambda E, sb_i=sb_i, n=sc_n: E.tensor_copy(
                    dq_scb[sb_i][:, 0:n], dq_scw[sb_i][:, 0:n]), sem="cV")
            lw = d["ncols"] // 4
            if kind == "gu":
                src16 = dq_pk[db].bitcast(I16)[:, 0:512]
            else:
                src16 = dq_pk2[db].bitcast(I16)[:, 352 * d["q"]:352 * (d["q"] + 1)]
            P.wait("vector", "cS", dqst["vAct"][db])
            P.wait("vector", "cP", max(dqst["vSgnP"]))
            for kk in range(4):
                P.op("vector", lambda E, kk=kk, db=db, src16=src16, lw=lw: E.tensor_scalar(
                    dq_m[db][:, kk * lw:(kk + 1) * lw], src16, 4 * kk, 7,
                    OP.logical_shift_right, OP.bitwise_and))
                P.op("vector", lambda E, kk=kk, db=db, src16=src16, lw=lw: E.tensor_scalar(
                    dq_s[db][:, kk * lw:(kk + 1) * lw], src16, 4 * kk + 3, 1,
                    OP.logical_shift_right, OP.bitwise_and))
            dqst["vX"][db] = P.cnt["cV"]

        def emit_dqACT(d):
            db, n = d["db"], d["ncols"]
            P.wait("scalar", "cV", dqst["vX"][db])
            P.wait("scalar", "cP", dqst["vP46"][1 - db])
            P.wait("scalar", "cV", dqst["vWop"][db])
            P.op("scalar", lambda E, db=db, n=n: E.activation(
                dq_r4[:, 0:n], dq_m[db][:, 0:n], AF.Relu, bias=f32(-2.0), scale=f32(0.5)))
            vA = P.op("scalar", lambda E, db=db, n=n: E.activation(
                dq_r6[:, 0:n], dq_m[db][:, 0:n], AF.Relu, bias=f32(-6.0), scale=f32(1.0)))
            dqst["vAct"][db] = vA

        def emit_dqPool(d):
            db, n = d["db"], d["ncols"]
            P.wait("gpsimd", "cV", dqst["vX"][db])
            P.wait("gpsimd", "cV", dqst["vWop"][db])
            vSgn = P.op("gpsimd", lambda E, db=db, n=n: E.tensor_scalar(
                dq_sg[db][:, 0:n], dq_s[db][:, 0:n], -2.0, 1.0, OP.mult, OP.add))
            dqst["vSgnP"][db] = vSgn
            P.wait("gpsimd", "cS", dqst["vAct"][db])
            P.wait("gpsimd", "cV", dqst["vT1"][db])
            vP46 = P.op("gpsimd", lambda E, db=db, n=n: E.tensor_tensor(
                dq_r46[db][:, 0:n], dq_r4[:, 0:n], dq_r6[:, 0:n], OP.add))
            dqst["vP46"][db] = vP46

        def emit_dqB(d):
            db, n, kind, sb_i = d["db"], d["ncols"], d["kind"], d["sb_i"]
            P.wait("vector", "cP", max(dqst["vP46"]))
            P.wait("vector", d["sDwrN"], dqst["vW_dma"][db])
            gofs = 0 if kind == "gu" else 11 * d["q"]
            sg4 = dq_sg[db][:, 0:n].rearrange("p (k g r) -> p k g r", k=4, r=32)
            scb = bass.AP(dq_scb[sb_i], gofs, [[44, 128], [0, 4], [1, d["ngl"]], [0, 32]])
            P.op("vector", lambda E, sg4=sg4, scb=scb: E.tensor_tensor(sg4, sg4, scb, OP.mult))
            P.op("vector", lambda E, db=db, n=n: E.scalar_tensor_tensor(
                dq_t1[:, 0:n], dq_m[db][:, 0:n], 0.5, dq_r46[db][:, 0:n], OP.mult, OP.add))
            dqst["vT1"][db] = P.cnt["cV"]
            vW = P.op("vector", lambda E, db=db, n=n: E.tensor_tensor(
                dq_w[db][:, 0:n], dq_t1[:, 0:n], dq_sg[db][:, 0:n], OP.mult))
            dqst["vWop"][db] = vW
            P.wait("sync", "cV", vW)
            r0 = d["r0"]
            if kind == "gu":
                dqst["vW_dma"][db] = P.dma("sync", dgu[r0:r0 + 128, :], dq_w[db][:, :],
                                           d["sDwrN"])
            else:
                dqst["vW_dma"][db] = P.dma(
                    "sync", ddw[r0:r0 + 128, d["q"] * 1408:(d["q"] + 1) * 1408],
                    dq_w[db][:, 0:1408], d["sDwrN"])
            dqst["wmark"].append((P.cnt["sDwr0"], P.cnt["sDwr1"]))

        def emit_dequant_chunk(idx):
            if idx >= 152:
                return
            d = dq_desc(idx)
            emit_dqA(d)
            emit_dqACT(d)
            emit_dqPool(d)
            emit_dqB(d)

        def pump_dequant(n):
            for _ in range(n):
                if dqst["i"] < 152:
                    emit_dequant_chunk(dqst["i"])
                    dqst["i"] += 1

        def pump_slot(n=1):
            for _ in range(n):
                if dqst["cur"] is None:
                    if dqst["i"] < 152:
                        dqst["cur"] = [dq_desc(dqst["i"]), 0]
                        dqst["i"] += 1
                    elif dqst["lagB"]:
                        emit_dqB(dqst["lagB"].pop(0))
                        continue
                    else:
                        return
                d, st = dqst["cur"]
                if st == 0:
                    emit_dqA(d)
                    dqst["cur"][1] = 1
                elif st == 1:
                    emit_dqACT(d)
                    dqst["cur"][1] = 2
                else:
                    emit_dqPool(d)
                    if dqst["lagB"]:
                        emit_dqB(dqst["lagB"].pop(0))
                    dqst["lagB"].append(d)
                    dqst["cur"] = None

        def pump_flush():
            while dqst["cur"] is not None or dqst["lagB"]:
                if dqst["cur"] is not None:
                    d, st = dqst["cur"]
                    (emit_dqA, emit_dqACT, emit_dqPool)[st](d)
                    dqst["cur"][1] = st + 1
                    if st == 2:
                        dqst["cur"] = None
                        dqst["lagB"].append(d)
                else:
                    emit_dqB(dqst["lagB"].pop(0))

        rst = {"vDve": 0, "vRaw": 0, "vTab": 0, "vRawB": {0: 0, 2: 0}}
        nst = {"vMM": 0, "vRed": 0, "vSrc": None}

        def emit_rope(bank, vMM, dst, dst_off, width=512):
            P.wait("scalar", "cT", vMM)
            P.wait("scalar", "cV", rst["vDve"])
            vRaw = P.op("scalar", lambda E, bank=bank, width=width: E.activation(
                rope_bf[:, 0:width], pbk[bank][:, 0:width], AF.Identity), sem="cS")
            rst["vRawB"][bank] = vRaw
            P.wait("tensor", "cS", vRaw)
            P.wait("tensor", "cV", rst["vDve"])
            P.wait("tensor", "sCst", vCst)
            vPq = P.op("tensor", lambda E, width=width: E.matmul(
                pbk[3][:, 0:width], cst_pw[:, :], rope_bf[:, 0:width], start=True, stop=True),
                sem="cT")
            P.wait("vector", "cT", vPq)
            P.wait("vector", "sRope", rst["vTab"])
            P.op("vector", lambda E, dst=dst, dst_off=dst_off, width=width: E.tensor_tensor(
                dst[:, dst_off:dst_off + width], pbk[3][:, 0:width], rbS[:, 0:width], OP.mult))
            P.op("vector", lambda E, width=width: E.tensor_tensor(
                rope_bf[:, 0:width], rope_bf[:, 0:width], rbC[:, 0:width], OP.mult))
            vD = P.op("vector", lambda E, dst=dst, dst_off=dst_off, width=width: E.tensor_tensor(
                dst[:, dst_off:dst_off + width], dst[:, dst_off:dst_off + width],
                rope_bf[:, 0:width], OP.add), sem="cV")
            rst["vDve"] = vD
            rst["vRaw"] = vRaw
            return vRaw, vD

        def emit_norm(col, src, src_off, width=512):
            P.wait("scalar", "cV", nst["vSrc"] if nst["vSrc"] is not None else P.cnt["cV"])
            nst["vSrc"] = None
            P.wait("scalar", "cT", nst["vMM"])
            vSq = P.op("scalar", lambda E, src=src, src_off=src_off, width=width: E.activation(
                rb2[:, 0:width], src[:, src_off:src_off + width], AF.Square), sem="cS")
            P.wait("tensor", "cS", vSq)
            P.wait("tensor", "cV", nst["vRed"])
            vMM = P.op("tensor", lambda E, width=width: E.matmul(
                pbk[4][0:1, 0:width], cst_on[:, 0:1], rb2[:, 0:width], start=True, stop=True),
                sem="cT")
            nst["vMM"] = vMM
            P.wait("vector", "cT", vMM)
            P.op("vector", lambda E, width=width: E.tensor_reduce(
                ntmp[0:1, 0:1], pbk[4][0:1, 0:width], AX, OP.max))
            vRed = P.op("vector", lambda E, col=col: E.tensor_tensor(
                nrm[0:1, col:col + 1], nrm[0:1, col:col + 1], ntmp[0:1, 0:1], OP.max), sem="cV")
            nst["vRed"] = vRed

        lst = {"vH": [0, 0], "vTp": [0, 0], "vSq": [0, 0], "vMMk": 0,
               "vMMv": 0, "vEvV": 0, "vMMq": 0, "tile": 0, "vHdma": [0, 0], "vTevac": 0,
               "vOutD": [0, 0], "vTevacP": [0, 0], "vWb": [0, 0],
               "pk": 0, "pv": 0, "pq": 0}

        def emit_ln_tile(bb, tt, which):
            par = lst["tile"] % 2
            base = (0 if which == 1 else 8)
            r0 = bb * 512 + tt * 128
            xt = xin[par]
            cS = stat[:, base:base + 1]
            cSS = stat[:, base + 1 + par:base + 2 + par]
            cMean = stat[:, base + 3 + par:base + 4 + par]
            cVar = stat[:, base + 5:base + 6]
            cRstd = stat[:, base + 6:base + 7]
            semH = "cV"
            P.wait("sync", semH, lst["vH"][par])
            P.wait("sync", "cS", lst["vSq"][par])
            if which == 2:
                P.wait("sync", "sOutF%d" % par, lst["vOutD"][par])
            sld = ("sXin%d" if which == 1 else "sX2r%d") % par
            if which == 1:
                vL = P.dma("sync", xt[:, :], xb[r0:r0 + 128, :], sld)
            else:
                vL = P.dma("sync", xt[:, :], x2d[r0:r0 + 128, :], sld)
            P.wait("vector", sld, vL)
            P.op("vector", lambda E, xt=xt, cS=cS: E.tensor_reduce(cS, xt[:, :], AX, OP.add))
            P.op("vector", lambda E, cS=cS, cMean=cMean: E.tensor_scalar(
                cMean, cS, 1.0 / M, None, OP.mult))
            P.wait("scalar", sld, vL)
            P.wait("scalar", semH, lst["vH"][1 - par])
            vSq = P.op("scalar", lambda E, xt=xt, cSS=cSS: E.activation(
                sqs[:, :], xt[:, :], AF.Square, accum_out=cSS), sem="cS")
            lst["vSq"][par] = vSq
            P.wait("vector", "cS", vSq)
            P.op("vector", lambda E, cMean=cMean, cVar=cVar: E.tensor_tensor(
                cVar, cMean, cMean, OP.mult))
            vV = P.op("vector", lambda E, cSS=cSS, cVar=cVar: E.scalar_tensor_tensor(
                cVar, cSS, 1.0 / M, cVar, OP.mult, OP.subtract), sem="cV")
            P.wait("scalar", "cV", vV)
            P.op("scalar", lambda E, cVar=cVar: E.activation(cVar, cVar, AF.Ln, bias=EPS_LN))
            vR = P.op("scalar", lambda E, cVar=cVar, cRstd=cRstd: E.activation(
                cRstd, cVar, AF.Exp, scale=f32(-0.5)), sem="cS")
            h1b = ht["b"][par]
            P.wait("vector", "cS", vR)
            P.wait("vector", "cV", vBcSaCa)
            P.wait("vector", "cT", lst["vTp"][par])
            P.wait("vector", "sDbg%d" % par, lst["vHdma"][par])
            ca_t = bc["ca1"] if which == 1 else bc["cb1"]
            sa_t = bc["sa"] if which == 1 else bc["sbv"]
            P.op("vector", lambda E, xt=xt, cMean=cMean, cRstd=cRstd: E.tensor_scalar(
                sqs[:, :], xt[:, :], cMean, cRstd, OP.subtract, OP.mult))
            P.op("vector", lambda E, ca_t=ca_t: E.tensor_tensor(sqs[:, :], sqs[:, :], ca_t[:, :], OP.mult))
            if which == 2:
                h_dst = h1b[:, :].rearrange("p (k g r) -> p g r k", k=4, r=32)
            else:
                h_dst = h1b[:, :]
            vH = P.op("vector", lambda E, h_dst=h_dst, sa_t=sa_t: E.tensor_tensor(
                h_dst, sqs[:, :], sa_t[:, :], OP.add), sem=semH)
            lst["vH"][par] = vH
            lst["tile"] += 1
            return vH, h1b, par

        def emit_transposes(h1b, vH, tt, par, pi_m, dstfm, semH):
            P.wait("tensor", semH, vH)
            P.wait("tensor", "sCst", vCst)
            P.wait("tensor", "cS", P.cnt["cS"])
            vTp_last = 0
            for j in range(NCH):
                pt = pT[j % 2]
                P.wait("tensor", "cV", lst["vTevacP"][j % 2])
                src = h1b[:, j * 128:(j + 1) * 128]
                vTp = P.op("tensor", lambda E, src=src, pt=pt: E.transpose(
                    pt, src, cst_id[:, :]), sem="cT")
                P.wait("vector", "cT", vTp)
                lst["vTevacP"][j % 2] = P.op("vector", lambda E, j=j, tt=tt, pt=pt, dstfm=dstfm: E.tensor_copy(
                    dstfm[:, j * 512 + tt * 128: j * 512 + tt * 128 + 128], pt), sem="cV")
                vTp_last = vTp
            lst["vTp"][par] = vTp_last

        def load_w3(dram3, idx, bi):
            src = dram3[idx].rearrange("(c p) d -> p c d", p=128)
            dst = wbuf[bi][:, :].rearrange("p (c d) -> p c d", c=NCH)
            return P.dma("gpsimd", dst, src, "sWkv%d" % bi)

        def emit_block_phase1(bb):
            P.wait("sync", "cV", rst["vDve"])
            P.dma("sync", rbC[:, :], rkC[:, bb * 512:(bb + 1) * 512], "sRope")
            rst["vTab"] = P.dma("sync", rbS[:, :], rkS[:, bb * 512:(bb + 1) * 512], "sRope")
            for tt in range(4):
                vH, h1b, par = emit_ln_tile(bb, tt, which=1)
                emit_transposes(h1b, vH, tt, par, pi_m=False, dstfm=h1fm, semH="cV")
                pump_slot(2)
                if "d_h1" in DBG:
                    P.wait("sync", "cV", vH)
                    lst["vHdma"][par] = P.dma(
                        "sync", DBG["d_h1"][bb * 512 + tt * 128: bb * 512 + tt * 128 + 128, :],
                        h1b[:, :], "sDbg%d" % par)
            vTe_blk = P.cnt["cV"]
            P.wait("gpsimd", "cT", lst["vWb"][0])
            lst["pk"] = load_w3(wk, 0, 0)
            P.wait("gpsimd", "cT", lst["vWb"][1])
            lst["pv"] = load_w3(wv, 0, 1)
            for g in range(G):
                vWk, vWv = lst["pk"], lst["pv"]
                P.wait("tensor", "cV", vTe_blk)
                P.wait("tensor", "sWkv0", vWk)
                P.wait("tensor", "cS", rst["vRaw"])

                def mmk(E):
                    r = None
                    for c in range(NCH):
                        r = E.matmul(pbk[0][:, :], wbuf[0][:, c * 128:(c + 1) * 128],
                                     h1fm[:, c * 512:(c + 1) * 512],
                                     start=(c == 0), stop=(c == NCH - 1))
                    return r
                vK = P.op("tensor", mmk, sem="cT")
                lst["vMMk"] = vK
                lst["vWb"][0] = vK
                P.wait("tensor", "sWkv1", vWv)
                P.wait("tensor", "cS", lst["vEvV"])

                def mmv(E):
                    r = None
                    for ts in range(4):
                        for c in range(NCH):
                            r = E.matmul(pbk[1][:, ts * 128:(ts + 1) * 128],
                                         h1fm[:, c * 512 + ts * 128: c * 512 + ts * 128 + 128],
                                         wbuf[1][:, c * 128:(c + 1) * 128],
                                         start=(c == 0), stop=(c == NCH - 1))
                    return r
                vV = P.op("tensor", mmv, sem="cT")
                lst["vMMv"] = vV
                lst["vWb"][1] = vV
                if g + 1 < G:
                    P.wait("gpsimd", "cT", lst["vWb"][0])
                    lst["pk"] = load_w3(wk, g + 1, 0)
                    P.wait("gpsimd", "cT", lst["vWb"][1])
                    lst["pv"] = load_w3(wv, g + 1, 1)
                pump_slot(3)
                emit_rope(0, vK, k_all, g * T + bb * 512)
                P.wait("scalar", "cT", vV)
                vEv = P.op("scalar", lambda E, g=g, bb=bb: E.activation(
                    v_all[:, (g * 16 + bb * 4) * 128:(g * 16 + bb * 4 + 4) * 128],
                    pbk[1][:, :], AF.Identity), sem="cS")
                lst["vEvV"] = vEv
            if bb < 2:
                pend = None
                P.wait("gpsimd", "cT", lst["vWb"][0])
                lst["pq"] = load_w3(wq, 0, 0)
                for hd in range(NHEAD):
                    bi = hd % 2
                    qbank = 2 if hd % 2 == 0 else 0
                    vWq = lst["pq"]
                    P.wait("tensor", "sWkv%d" % bi, vWq)
                    P.wait("tensor", "cV", vTe_blk)
                    P.wait("tensor", "cS", rst["vRawB"][qbank])

                    def mmq(E, bi=bi, qbank=qbank):
                        r = None
                        for c in range(NCH):
                            r = E.matmul(pbk[qbank][:, :], wbuf[bi][:, c * 128:(c + 1) * 128],
                                         h1fm[:, c * 512:(c + 1) * 512],
                                         start=(c == 0), stop=(c == NCH - 1))
                        return r
                    vQ = P.op("tensor", mmq, sem="cT")
                    lst["vMMq"] = vQ
                    lst["vWb"][bi] = vQ
                    if hd + 1 < NHEAD:
                        bi2 = (hd + 1) % 2
                        P.wait("gpsimd", "cT", lst["vWb"][bi2])
                        lst["pq"] = load_w3(wq, hd + 1, bi2)
                    _, vD = emit_rope(qbank, vQ, q_all, hd * TOWN + bb * 512)
                    if pend is not None:
                        emit_norm(*pend)
                    pend = (hd, q_all, hd * TOWN + bb * 512)
                    nst["vSrc"] = vD
                    pump_slot(3)
                emit_norm(*pend)
            for g in range(G):
                emit_norm(16 + g, k_all, g * T + bb * 512)

        for bb in range(4):
            emit_block_phase1(bb)
            jlo, jhi = 32 + bb * 16, min(32 + (bb + 1) * 16, 96)
            mst["pend"] = load_mod(jlo)
            for jt in range(jlo, jhi):
                vLm = mst["pend"]
                if jt + 1 < jhi:
                    mst["pend"] = load_mod(jt + 1)
                emit_mod_tile(jt, vLm)
                pump_slot(2)

        if "d_kall" in DBG:
            P.wait("sync", "cV", P.cnt["cV"])
            P.wait("sync", "cS", P.cnt["cS"])
            P.dma("sync", DBG["d_kall"][:, :], k_all[:, :], "sOut")
            P.dma("sync", DBG["d_vall"][:, :], v_all[:, :], "sOut")
            P.dma("sync", DBG["d_qall"][:, :], q_all[:, :], "sOut")
        if "d_bc" in DBG:
            P.wait("sync", "cV", P.cnt["cV"])
            for i, kk in enumerate(("ca1", "sa", "cb1", "sbv", "ga", "gb")):
                P.dma("sync", DBG["d_bc"][:, i * M:(i + 1) * M], bc[kk][:, :], "sOut")

        pump_flush()
        with nc.Block() as blk1:
            P.flush(blk1)
        ctx1.close()

        if upto >= 2:
            sb2 = lambda name, shape, dt: ctx2.enter_context(nc.sbuf_tensor(name, shape, dt))
            o_all = sb2("o_all", [128, NHEAD * TOWN], BF16)
            mskb = sb2("mskb", [128, 16 * 512], U8)
            p_raw = [sb2("p_raw%d" % i, [128, 512], BF16) for i in range(3)]
            p_msk = [sb2("p_msk%d" % i, [128, 512], BF16) for i in range(4)]
            acc_sg = sqs[:, 0:512]
            onesf = stat[:, 20:21]
            wo_buf = sb2("wo_buf", [128, NHEAD * 256], BF16)
            alpha_s = sqs[:, 1536:2048]
            x2s = sqs[:, 0:1024]
            r_ga = sqs[:, 1024:1536]
            abf_row = sqs[:, :].bitcast(BF16)[0:1, 3072:3584]

            vN = P.cnt["cV"]
            P.wait("vector", "cV", vN)
            P.op("vector", lambda E: E.tensor_reduce(nrm[0:1, 25:26], nrm[0:1, 0:16], AX, OP.max))
            P.op("vector", lambda E: E.tensor_reduce(nrm[0:1, 26:27], nrm[0:1, 16:20], AX, OP.max))
            vB0 = P.op("vector", lambda E: E.tensor_tensor(nrm[0:1, 27:28], nrm[0:1, 25:26],
                                                           nrm[0:1, 26:27], OP.mult), sem="cV")
            P.wait("scalar", "cV", vB0)
            P.op("scalar", lambda E: E.activation(nrm[0:1, 28:29], nrm[0:1, 27:28], AF.Ln,
                                                  bias=f32(1e-30)))
            vB1 = P.op("scalar", lambda E: E.activation(nrm[0:1, 29:30], nrm[0:1, 28:29], AF.Exp,
                                                        scale=f32(0.5)), sem="cS")
            P.wait("vector", "cS", vB1)
            P.op("vector", lambda E: E.tensor_scalar(nrm[0:1, 30:31], nrm[0:1, 29:30], 2.0, -1.0,
                                                     OP.add, OP.mult))
            vB2 = P.op("vector", lambda E: E.tensor_copy(sbrow[0:1, 250:251], nrm[0:1, 30:31]),
                       sem="cV")
            P.wait("tensor", "cV", vB2)
            vB3 = P.op("tensor", lambda E: E.matmul(pbk[4][:, 0:1], cst_on[0:1, :],
                                                    sbrow[0:1, 250:251], start=True, stop=True),
                       sem="cT")
            P.wait("scalar", "cT", vB3)
            vB4 = P.op("scalar", lambda E: E.activation(nBcol[:, :], pbk[4][:, 0:1], AF.Identity),
                       sem="cS")
            if "d_nrm" in DBG:
                P.wait("sync", "cS", vB4)
                P.dma("sync", DBG["d_nrm"][:, :], nrm[:, :], "sOut")

            vOnes = P.op("vector", lambda E: E.memset(onesf, 1.0), sem="cV")
            SBANK = (0, 1, 7)
            ast = {"vE3": [0, 0, 0], "vMk": [0, 0, 0], "vPV": 0, "vO": 0, "vMkLast": 0,
                   "vAe": 0, "vPVp": [0, 0, 0, 0], "vOp": [0, 0], "vSg": [0, 0], "vSgf": 0,
                   "vS": [0, 0, 0]}
            for qb in range(2):
                P.wait("sync", "cV", ast["vMkLast"])
                vM = P.dma("sync", mskb[:, :].rearrange("p (c q) -> p c q", c=16),
                           masks[:, :].rearrange("(b c p) q -> p b c q", b=2, p=128)[:, qb:qb + 1],
                           "sMsk")
                def emit_s(hd, cc, first=False):
                    g = hd // H
                    par = (hd * 16 + cc) % 3
                    sB = pbk[SBANK[par]]
                    q_off = hd * TOWN + qb * 512
                    P.wait("tensor", "cS", ast["vE3"][par])
                    if first:
                        P.wait("tensor", "cS", vB4)
                    vS = P.op("tensor", lambda E, g=g, cc=cc, q_off=q_off, sB=sB: E.matmul(
                        sB[:, :], k_all[:, g * T + cc * 128: g * T + cc * 128 + 128],
                        q_all[:, q_off:q_off + 512], start=True, stop=True), sem="cT")
                    ast["vS"][par] = vS

                for hd in range(NHEAD):
                    g = hd // H
                    hp = (qb * NHEAD + hd) % 2
                    pvB = pbk[2 if hp == 0 else 5]
                    sgB = pbk[3 if hp == 0 else 6]
                    vAccEnd = 0
                    dqs = [dq_desc(dqst["i"] + j) if dqst["i"] + j < 152 else None
                           for j in range(2)]
                    dqst["i"] = min(dqst["i"] + 2, 152)
                    dq_sched = {-1: ("A", 0), 2: ("ACT", 0), 5: ("P", 0),
                                8: ("A", 1), 10: ("ACT", 1), 13: ("P", 1)}

                    def dq_step(cc_key):
                        st = dq_sched.get(cc_key)
                        if st is not None and dqs[st[1]] is not None:
                            {"A": emit_dqA, "ACT": emit_dqACT, "P": emit_dqPool}[st[0]](
                                dqs[st[1]])
                    dq_step(-1)
                    if hd == 0:
                        for cc0 in range(3):
                            emit_s(hd, cc0, first=(cc0 == 0))
                    for cc in range(16):
                        dq_step(cc)
                        gcc = hd * 16 + cc
                        par = gcc % 3
                        sB = pbk[SBANK[par]]
                        pr = p_raw[gcc % 3]
                        pm = p_msk[gcc % 4]
                        P.wait("scalar", "cT", ast["vS"][par])
                        P.wait("scalar", "cV", ast["vMk"][gcc % 3])
                        P.wait("scalar", "cS", vB4)
                        vE = P.op("scalar", lambda E, sB=sB, pr=pr: E.activation(
                            pr[:, :], sB[:, :], AF.Exp, bias=nBcol[:, 0:1]), sem="cS")
                        ast["vE3"][par] = vE
                        P.wait("vector", "cS", vE)
                        P.wait("vector", "cT", ast["vPVp"][gcc % 4])
                        P.wait("vector", "sMsk", vM)
                        if cc == 0:
                            P.wait("vector", "cT", ast["vSgf"])
                        vMk = P.op("vector", lambda E, cc=cc, pm=pm, pr=pr: E.tensor_tensor(
                            pm[:, :], pr[:, :], mskb[:, cc * 512:(cc + 1) * 512], OP.mult),
                            sem="cV")
                        ast["vMk"][gcc % 3] = vMk
                        ast["vMkLast"] = vMk
                        if cc == 0:
                            vAcc = P.op("vector", lambda E, pm=pm: E.tensor_copy(
                                acc_sg, pm[:, :]), sem="cV")
                        else:
                            vAcc = P.op("vector", lambda E, pm=pm: E.tensor_tensor(
                                acc_sg, acc_sg, pm[:, :], OP.add), sem="cV")
                        vAccEnd = vAcc
                        P.wait("tensor", "cV", vMk)
                        if cc == 0:
                            P.wait("tensor", "cV", ast["vOp"][hp])
                        vPV = P.op("tensor", lambda E, pm=pm, g=g, cc=cc, pvB=pvB: E.matmul(
                            pvB[:, :], v_all[:, (g * 16 + cc) * 128:(g * 16 + cc + 1) * 128],
                            pm[:, :], start=(cc == 0), stop=(cc == 15)), sem="cT")
                        ast["vPVp"][gcc % 4] = vPV
                        ast["vPV"] = vPV
                        if cc + 3 < 16:
                            emit_s(hd, cc + 3)
                        elif hd < NHEAD - 1:
                            emit_s(hd + 1, cc - 13)
                    P.wait("tensor", "cV", max(vAccEnd, vOnes))
                    P.wait("tensor", "cS", ast["vSg"][hp])
                    vSgf = P.op("tensor", lambda E, sgB=sgB: E.matmul(
                        sgB[0:1, :], onesf, acc_sg, start=True, stop=True),
                        sem="cT")
                    ast["vSgf"] = vSgf
                    P.wait("scalar", "cT", vSgf)
                    P.wait("scalar", "cV", ast["vO"])
                    vAl = P.op("scalar", lambda E, sgB=sgB: E.activation(
                        srow[0:1, 0:512], sgB[0:1, :], AF.Ln, bias=f32(1e-30)), sem="cS")
                    ast["vSg"][hp] = vAl
                    P.op("scalar", lambda E: E.activation(srow[0:1, 0:512], srow[0:1, 0:512],
                                                          AF.Exp, scale=f32(-1.0)))
                    vAlE = P.cnt["cS"]
                    P.wait("vector", "cS", vAlE)
                    vAb = P.op("vector", lambda E: E.tensor_copy(abf_row,
                                                                 srow[0:1, 0:512]), sem="cV")
                    P.wait("tensor", "cV", vAb)
                    P.wait("tensor", "cS", ast["vAe"])
                    vAp = P.op("tensor", lambda E: E.matmul(pbk[4][:, :], cst_on[0:1, :],
                                                            abf_row, start=True, stop=True),
                               sem="cT")
                    P.wait("scalar", "cT", vAp)
                    vAe = P.op("scalar", lambda E: E.activation(alpha_s, pbk[4][:, :],
                                                                AF.Identity), sem="cS")
                    ast["vAe"] = vAe
                    P.wait("vector", "cS", vAe)
                    vO = P.op("vector", lambda E, hd=hd, qb=qb, pvB=pvB: E.tensor_tensor(
                        o_all[:, hd * TOWN + qb * 512: hd * TOWN + qb * 512 + 512],
                        pvB[:, :], alpha_s, OP.mult), sem="cV")
                    ast["vO"] = vO
                    ast["vOp"][hp] = vO
                    for dd in dqs:
                        if dd is not None:
                            emit_dqB(dd)

            if "d_oall" in DBG:
                P.wait("sync", "cV", P.cnt["cV"])
                P.dma("sync", DBG["d_oall"][:, :], o_all[:, :], "sOut")

            vO_all = P.cnt["cV"]
            P.wait("sync", "cV", P.cnt["cV"])
            P.wait("sync", "cS", P.cnt["cS"])
            rstt = {"vR": [0, 0], "vX2": [0, 0], "vSpill": [0, 0], "vRLast": 0}
            for mcg in range(8):
                P.wait("gpsimd", "cT", rstt["vRLast"])
                for hd in range(NHEAD):
                    P.dma("gpsimd", wo_buf[:, hd * 256:(hd + 1) * 256],
                          wo[hd, :, mcg * 256:(mcg + 1) * 256], "sWo")
                vWoL = P.cnt["sWo"]
                for tt in range(8):
                    par = tt % 2
                    P.wait("tensor", "sWo", vWoL)
                    P.wait("tensor", "cV", vO_all)
                    P.wait("tensor", "cV", rstt["vX2"][par])

                    def mmr(E, par=par, tt=tt):
                        r = None
                        for hd in range(NHEAD):
                            r = E.matmul(pbk[5 + par][:, 0:256],
                                         o_all[:, hd * TOWN + tt * 128: hd * TOWN + tt * 128 + 128],
                                         wo_buf[:, hd * 256:(hd + 1) * 256],
                                         start=(hd == 0), stop=(hd == NHEAD - 1))
                        return r
                    vR = P.op("tensor", mmr, sem="cT")
                    rstt["vR"][par] = vR
                    rstt["vRLast"] = vR
                    P.wait("sync", "cV", rstt["vX2"][par])
                    vXr = P.dma("sync", xin[par][:, mcg * 256:(mcg + 1) * 256],
                                xb[tt * 128:(tt + 1) * 128, mcg * 256:(mcg + 1) * 256],
                                "sXr%d" % par)
                    P.wait("vector", "cT", vR)
                    P.wait("vector", "sXr%d" % par, vXr)
                    P.wait("vector", "sX2w%d" % par, rstt["vSpill"][par])
                    P.op("vector", lambda E, mcg=mcg, par=par: E.tensor_tensor(
                        r_ga[:, 0:256], pbk[5 + par][:, 0:256],
                        bc["ga"][:, mcg * 256:(mcg + 1) * 256], OP.mult))
                    vX2 = P.op("vector", lambda E, mcg=mcg, par=par: E.tensor_tensor(
                        x2s[:, par * 256:par * 256 + 256], xin[par][:, mcg * 256:(mcg + 1) * 256],
                        r_ga[:, 0:256], OP.add), sem="cV")
                    rstt["vX2"][par] = vX2
                    P.wait("sync", "cV", vX2)
                    vSp = P.dma("sync", x2d[tt * 128:(tt + 1) * 128, mcg * 256:(mcg + 1) * 256],
                                x2s[:, par * 256:par * 256 + 256], "sX2w%d" % par)
                    rstt["vSpill"][par] = vSp
                pump_dequant(2)
            pump_dequant(8) if True else None
            if "d_x2" in DBG:
                P.wait("sync", "sX2w0", P.cnt["sX2w0"])
                P.wait("sync", "sX2w1", P.cnt["sX2w1"])
                for tt in range(8):
                    P.dma("sync", DBG["d_x2"][tt * 128:(tt + 1) * 128, :],
                          x2d[tt * 128:(tt + 1) * 128, :], "sOut")

            with nc.Block() as blk2:
                P.flush(blk2)
            ctx2.close()
            ctx_attn.close()

        if upto >= 3:
            sb3 = lambda name, shape, dt: ctx3.enter_context(nc.sbuf_tensor(name, shape, dt))
            pump_dequant(200)
            for eng in ENGS:
                P.wait(eng, "sDwr0", P.cnt["sDwr0"])
                P.wait(eng, "sDwr1", P.cnt["sDwr1"])
                P.wait(eng, "sPk", P.cnt["sPk"])
                P.wait(eng, "sPk1", P.cnt["sPk1"])
            ctx_dq.close()
            P.wait("sync", "sDbg0", P.cnt["sDbg0"])
            P.wait("sync", "sDbg1", P.cnt["sDbg1"])
            for vi, vec in enumerate(("sbv", "cb1", "gb")):
                bc[vec] = sb3("bc3_" + vec, [128, M], BF16)
                vBc3 = P.dma("sync", bc[vec][:, :], bcd[vi], "sOut")
            for eng in ("vector", "scalar"):
                P.wait(eng, "sOut", vBc3)
            ht["b"] = [sb3("h2t%d" % i, [128, M], BF16) for i in range(2)]
            h2fm = sb3("h2fm", [128, NCH * 512], BF16)
            act_all = sb3("act_all", [128, 44 * 512], BF16)
            sg_s = sb3("sg_s", [128, 512], BF16)
            wguP = [sb3("wguP%d" % i, [128, NCH * 1024], BF16) for i in range(2)]
            y_sb = wguP[0].bitcast(F32)
            wdn = [sb3("wdn%d" % i, [128, 1024], BF16) for i in range(4)]
            P.wait("scalar", "sDwr0", P.cnt["sDwr0"])
            P.wait("scalar", "sDwr1", P.cnt["sDwr1"])
            P.wait("sync", "sDwr0", P.cnt["sDwr0"])
            P.wait("sync", "sDwr1", P.cnt["sDwr1"])
            for eng in ("vector", "scalar", "sync", "gpsimd"):
                P.wait(eng, "sX2w0", P.cnt["sX2w0"])
                P.wait(eng, "sX2w1", P.cnt["sX2w1"])
                P.wait(eng, "sXr0", P.cnt["sXr0"])
                P.wait(eng, "sXr1", P.cnt["sXr1"])
                P.wait(eng, "sOut", P.cnt["sOut"])

            if "d_dgu" in DBG:
                for rt in range(88):
                    P.dma("sync", DBG["d_dgu"][rt * 128:(rt + 1) * 128, :],
                          dgu[rt * 128:(rt + 1) * 128, :], "sOut")
            if "d_ddw" in DBG:
                for rt in range(16):
                    P.dma("sync", DBG["d_ddw"][rt * 128:(rt + 1) * 128, :],
                          ddw[rt * 128:(rt + 1) * 128, :], "sOut")

            fst = {"vGu": [0, 0], "vActH": [0, 0], "vSgH": [0, 0], "vDn": [0, 0, 0, 0], "vY": 0,
                   "vFin": [0, 0], "vDbgAct": 0, "vWgP": [0, 0]}
            lst["vH"] = [0, 0]
            lst["vSq"] = [P.cnt["cS"], P.cnt["cS"]]
            for t5 in range(2):
                for tt in range(4):
                    vH, h2b, par = emit_ln_tile(t5, tt, which=2)
                    emit_transposes(h2b, vH, tt, par, pi_m=True, dstfm=h2fm, semH="cV")
                vH2blk = P.cnt["cV"]

                P.wait("vector", "sOut", fst["vDbgAct"])
                if t5 == 1:
                    P.wait("scalar", "cV", max(fst["vFin"]))
                for og in range(22):
                    p, bp = og // 2, (og // 2) % 2
                    if og % 2 == 0:
                        P.wait("scalar", "cT", fst["vGu"][bp])
                        for c in range(NCH):
                            P.dma("scalar", wguP[bp][:, c * 1024:(c + 1) * 1024],
                                  dgu[p * 1024:(p + 1) * 1024, c * 128:(c + 1) * 128],
                                  "sGuSc%d" % bp, transpose=True)
                        fst["vWgP"][bp] = P.cnt["sGuSc%d" % bp]
                    vWg = fst["vWgP"][bp]
                    obase = (og % 2) * 512
                    for half in range(2):
                        gate_off = obase + half * 256
                        up_off = obase + half * 256 + 128
                        P.wait("tensor", "sGuSc%d" % bp, vWg)
                        P.wait("tensor", "cV", vH2blk)
                        P.wait("tensor", "cV", fst["vActH"][half])
                        P.wait("tensor", "cS", fst["vSgH"][half])

                        def mmgu(E, bp=bp, gate_off=gate_off, up_off=up_off, half=half):
                            r = None
                            for c in range(NCH):
                                r = E.matmul(pbk[half * 2][:, :],
                                             wguP[bp][:, c * 1024 + gate_off: c * 1024 + gate_off + 128],
                                             h2fm[:, c * 512:(c + 1) * 512],
                                             start=(c == 0), stop=(c == NCH - 1))
                            for c in range(NCH):
                                r = E.matmul(pbk[half * 2 + 1][:, :],
                                             wguP[bp][:, c * 1024 + up_off: c * 1024 + up_off + 128],
                                             h2fm[:, c * 512:(c + 1) * 512],
                                             start=(c == 0), stop=(c == NCH - 1))
                            return r
                        vGu = P.op("tensor", mmgu, sem="cT")
                        if half == 1 and og % 2 == 1:
                            fst["vGu"][bp] = vGu
                        ic = og * 2 + half
                        P.wait("scalar", "cT", vGu)
                        P.wait("scalar", "cV", max(fst["vActH"]))
                        vSg = P.op("scalar", lambda E, half=half: E.activation(
                            sg_s[:, :], pbk[half * 2][:, :], AF.Silu), sem="cS")
                        fst["vSgH"][half] = vSg
                        P.wait("vector", "cS", vSg)
                        vAct = P.op("vector", lambda E, ic=ic, half=half: E.tensor_tensor(
                            act_all[:, ic * 512:(ic + 1) * 512], sg_s[:, :],
                            pbk[half * 2 + 1][:, :], OP.mult), sem="cV")
                        fst["vActH"][half] = vAct
                vGuAll = P.cnt["cV"]
                if "d_act" in DBG:
                    P.wait("sync", "cV", vGuAll)
                    fst["vDbgAct"] = P.dma("sync", DBG["d_act"][:, t5 * 44 * 512:(t5 + 1) * 44 * 512],
                                           act_all[:, :], "sOut")

                for hp2 in range(2):
                    if t5 == 0:
                        pump_dequant(120 + 32 * hp2 - dqst["i"])
                        mk = dqst["wmark"][min(120 + 32 * hp2, 152) - 1]
                        P.wait("sync", "sDwr0", mk[0])
                        P.wait("sync", "sDwr1", mk[1])
                    for ic in range(44):
                        wbi = ic % 4
                        P.wait("sync", "cT", fst["vDn"][wbi])
                        vWd = P.dma("sync", wdn[wbi][:, :],
                                    ddw[hp2 * 1024:(hp2 + 1) * 1024, ic * 128:(ic + 1) * 128],
                                    "sDwSt%d" % wbi, transpose=True)
                        P.wait("tensor", "sDwSt%d" % wbi, vWd)
                        P.wait("tensor", "cV", vGuAll)
                        if ic == 0:
                            P.wait("tensor", "cS", fst["vY"])

                        def mmd(E, ic=ic, wbi=wbi):
                            r = None
                            for tt in range(4):
                                for h2 in range(2):
                                    r = E.matmul(pbk[tt * 2 + h2][:, :],
                                                 act_all[:, ic * 512 + tt * 128: ic * 512 + tt * 128 + 128],
                                                 wdn[wbi][:, h2 * 512:(h2 + 1) * 512],
                                                 start=(ic == 0), stop=(ic == 43))
                            return r
                        vDn = P.op("tensor", mmd, sem="cT")
                        fst["vDn"][wbi] = vDn
                    vDnl = P.cnt["cT"]
                    P.wait("scalar", "cT", vDnl)
                    P.wait("scalar", "cV", max(fst["vFin"]))
                    vY = 0
                    for tt in range(4):
                        for h2 in range(2):
                            vY = P.op("scalar", lambda E, tt=tt, h2=h2, hp2=hp2: E.activation(
                                y_sb[:, tt * M + hp2 * 1024 + h2 * 512:
                                     tt * M + hp2 * 1024 + h2 * 512 + 512],
                                pbk[tt * 2 + h2][:, :], AF.Identity), sem="cS")
                    fst["vY"] = vY

                vYall = P.cnt["cS"]
                for tt in range(4):
                    par = tt % 2
                    r0 = t5 * 512 + tt * 128
                    ysl = y_sb[:, tt * M:(tt + 1) * M]
                    cMs = stat[:, 16 + par:17 + par]
                    cRs = stat[:, 18 + par:19 + par]
                    P.wait("scalar", "cS", vYall)
                    P.wait("scalar", "cV", max(fst["vFin"]))
                    vSq = P.op("scalar", lambda E, ysl=ysl, cMs=cMs: E.activation(
                        sqs[:, :], ysl, AF.Square, accum_out=cMs), sem="cS")
                    P.wait("scalar", "cS", vSq)
                    P.op("scalar", lambda E, cMs=cMs: E.activation(cMs, cMs, AF.Ln, bias=EPS_RMS,
                                                                   scale=f32(1.0 / M)))
                    vRs = P.op("scalar", lambda E, cMs=cMs, cRs=cRs: E.activation(
                        cRs, cMs, AF.Exp, scale=f32(-0.5)), sem="cS")
                    xt = xin[par]
                    P.wait("sync", "cV", fst["vFin"][par])
                    P.wait("sync", "sOutF%d" % par, lst["vOutD"][par])
                    vL = P.dma("sync", xt[:, :], x2d[r0:r0 + 128, :], "sX2r%d" % par)
                    P.wait("vector", "cS", vRs)
                    P.wait("vector", "sX2r%d" % par, vL)
                    P.op("vector", lambda E, ysl=ysl, cRs=cRs: E.tensor_scalar(
                        sqs[:, :], ysl, cRs, None, OP.mult))
                    P.op("vector", lambda E: E.tensor_tensor(sqs[:, :], sqs[:, :], bc["gb"][:, :],
                                                             OP.mult))
                    vF = P.op("vector", lambda E, xt=xt: E.tensor_tensor(
                        xt[:, :], xt[:, :], sqs[:, :], OP.add), sem="cV")
                    fst["vFin"][par] = vF
                    P.wait("sync", "cV", vF)
                    lst["vOutD"][par] = P.dma("sync", xout[r0:r0 + 128, :], xt[:, :],
                                              "sOutF%d" % par)

        vOut = P.cnt["sOut"]
        P.wait("sync", "sOut", vOut)
        P.wait("sync", "sOutF0", P.cnt["sOutF0"])
        P.wait("sync", "sOutF1", P.cnt["sOutF1"])
        P.wait("sync", "sDbg0", P.cnt["sDbg0"])
        P.wait("sync", "sDbg1", P.cnt["sDbg1"])
        P.wait("sync", "sDwr0", P.cnt["sDwr0"])
        P.wait("sync", "sDwr1", P.cnt["sDwr1"])
        P.wait("sync", "sX2w0", P.cnt["sX2w0"])
        P.wait("sync", "sX2w1", P.cnt["sX2w1"])

        with nc.Block() as blk:
            P.flush(blk)
        ctx3.close()
        ctx2.close()
        ctx_attn.close()
        ctx_dq.close()
        ctx1.close()

    return nc


def _prep_inputs(inputs):
    x = np.ascontiguousarray(inputs["L0_attn_x"])
    w_aq = np.ascontiguousarray(inputs["L0_attn_w_aq"]).reshape(NHEAD, M, D)
    w_ak = np.ascontiguousarray(inputs["L0_attn_w_ak"])
    w_av = np.ascontiguousarray(inputs["L0_attn_w_av"])
    w_ao = np.ascontiguousarray(inputs["L0_attn_w_ao"]).reshape(NHEAD, D, M)
    gup = np.ascontiguousarray(inputs["L0_ffn_gate_up_packed"])
    gus = np.ascontiguousarray(inputs["L0_ffn_gate_up_scales"])
    dwp = np.ascontiguousarray(inputs["L0_ffn_down_packed"])
    dws = np.ascontiguousarray(inputs["L0_ffn_down_scales"])
    c = np.ascontiguousarray(inputs["L0_c"])
    wmod = np.ascontiguousarray(inputs["L0_wmod"])

    permI = perm_dwq(I)
    row_order = np.empty(GU_ROWS, dtype=np.int64)
    for cblk in range(44):
        ii = permI[cblk * 128:(cblk + 1) * 128]
        row_order[cblk * 256:cblk * 256 + 128] = ii
        row_order[cblk * 256 + 128:cblk * 256 + 256] = I + ii
    gup_r = np.ascontiguousarray(gup[row_order])
    gusT_r = np.ascontiguousarray(gus.T[row_order])
    dwsT = np.ascontiguousarray(dws.T)

    identb = np.eye(128, dtype=ml_dtypes.bfloat16)
    onesb = np.ones((128, 128), dtype=ml_dtypes.bfloat16)
    pswap = np.zeros((128, 128), dtype=np.float32)
    pswap[np.arange(64), np.arange(64) + 64] = 1
    pswap[np.arange(64) + 64, np.arange(64)] = 1
    pswap = pswap.astype(ml_dtypes.bfloat16)

    in_maps = []
    for core in range(8):
        b = core // 2
        own = core % 2
        perm = np.concatenate([np.arange(own * TOWN, (own + 1) * TOWN),
                               np.arange((1 - own) * TOWN, (2 - own) * TOWN)])
        xb_c = np.ascontiguousarray(x[b][perm])
        kC, kS = rope_tables(perm.astype(np.float64))
        msk = causal_masks(perm).reshape(2 * 16 * 128, 512)
        cT_c = np.ascontiguousarray(c[b].reshape(16, 128).T)
        in_maps.append(dict(
            xb=xb_c, wq=w_aq, wk=w_ak, wv=w_av, wo=w_ao,
            gup=gup_r, gusT=gusT_r, dwp=dwp, dwsT=dwsT,
            cT=cT_c, wmod=wmod,
            rkC=kC, rkS=kS,
            masks=np.ascontiguousarray(msk), identb=identb, onesb=onesb, pswap=pswap,
        ))
    return in_maps


def kernel(**inputs):
    in_maps = _prep_inputs(inputs)
    nc = build_program(upto=3)
    res = run_bass_kernel_spmd(nc, in_maps, core_ids=list(range(8)))
    out = np.empty((B, T, M), dtype=np.float32)
    for core in range(8):
        b = core // 2
        own = core % 2
        out[b, own * TOWN:(own + 1) * TOWN, :] = res.results[core]["xout"]
    return out
